# version 09
# baseline (speedup 1.0000x reference)
import numpy as np
import ml_dtypes
from contextlib import ExitStack
import concourse.bass as bass
import concourse.mybir as mybir
from concourse.bass_utils import run_bass_kernel_spmd

F32 = mybir.dt.float32
BF16 = mybir.dt.bfloat16
AF = mybir.ActivationFunctionType
ALU = mybir.AluOpType
AXX = mybir.AxisListType.X

D = 4096
HM, PM, GM, NM, KCONV, CH = 128, 64, 8, 128, 4, 128
DIN = HM * PM
CONVD = DIN + 2 * GM * NM
PROJ = DIN + CONVD + HM
DC = D // 128
NOC = PROJ // 128
OC_HBC0, OC_DT = 64, 144
EPS = 1e-5
NB = 3


def banks_of(total, width=512, off=0):
    out = []
    s = 0
    while s < total:
        w = min(width, total - s)
        out.append((off + s, w))
        s += w
    return out


class Chain:

    def __init__(self, eng, sem):
        self.e, self.sem, self.n = eng, sem, 0
        self.w = {}

    def dep(self, tok):
        if tok is None:
            return
        if isinstance(tok, (list, tuple)) and len(tok) and isinstance(tok[0], (list, tuple, type(None))):
            for t in tok:
                self.dep(t)
            return
        sem, val = tok
        if val <= 0:
            return
        if sem is self.sem:
            return
        if self.w.get(sem.num, -1) >= val:
            return
        self.e.wait_ge(sem, val)
        self.w[sem.num] = val

    def op(self, build, deps=()):
        for d in deps:
            self.dep(d)
        i = build(self.e)
        if self.n:
            i._wait_ge(self.sem, self.n)
        i.then_inc(self.sem, 1)
        self.n += 1
        return (self.sem, self.n)

    def tok(self):
        return (self.sem, self.n)


class Pe(Chain):

    def __init__(self, eng, sem):
        super().__init__(eng, sem)
        self.last = None

    def mm(self, out, lhsT, rhs, start, stop, deps=()):
        for d in deps:
            self.dep(d)
        self.last = self.e.matmul(out, lhsT, rhs, start=start, stop=stop)
        return self.last

    def tr(self, out, in_, ident, deps=()):
        for d in deps:
            self.dep(d)
        self.last = self.e.transpose(out, in_, ident)
        return self.last

    def mark(self):
        self.last.then_inc(self.sem, 1)
        self.n += 1
        return (self.sem, self.n)

    def op(self, *a, **k):
        raise RuntimeError("use mm/tr on Pe")


class Dma:

    def __init__(self, bld, eng, prefix):
        self.bld, self.e, self.prefix = bld, eng, prefix
        self.slots = {}
        self.w = {}

    def _slot(self, name):
        if name not in self.slots:
            self.slots[name] = [self.bld.sem(f"{self.prefix}_{name}"), 0]
        return self.slots[name]

    def dep(self, tok):
        if tok is None:
            return
        if isinstance(tok, (list, tuple)) and len(tok) and isinstance(tok[0], (list, tuple, type(None))):
            for t in tok:
                self.dep(t)
            return
        sem, val = tok
        if val <= 0:
            return
        if self.w.get(sem.num, -1) >= val:
            return
        self.e.wait_ge(sem, val)
        self.w[sem.num] = val

    def go(self, slot, out, in_, deps=(), fresh=True, **kw):
        sl = self._slot(slot)
        if fresh and sl[1]:
            self.dep((sl[0], sl[1]))
        for d in deps:
            self.dep(d)
        self.e.dma_start(out=out, in_=in_, **kw).then_inc(sl[0], 16)
        sl[1] += 16
        return (sl[0], sl[1])

    def tok(self, slot):
        sl = self._slot(slot)
        return (sl[0], sl[1])


class Build:
    def __init__(self, tok=2048, taps=(), n_cores=8, stop_after=None, naa=None,
                 pre=128, dbg=()):
        self.dbg = set(dbg)
        self.stop_after = stop_after
        self.TOK = tok
        self.naa = naa
        self.PRE = pre
        self.NPRE = pre // CH
        self.TOKA = tok + pre
        self.T = self.TOKA + NB
        self.OWN0 = NB + pre
        self.NCH = tok // CH
        self.NCHA = self.TOKA // CH
        self.taps = set(taps)
        self.n_cores = n_cores
        self.nc = bass.Bass("TRN2", target_bir_lowering=False)
        self.ext_in = {}
        self.ext_out = {}
        self._root = ExitStack()

    def inp(self, name, shape, dt=F32):
        t = self.nc.dram_tensor(name, list(shape), dt, kind="ExternalInput")
        self.ext_in[name] = t
        return t.ap()

    def outp(self, name, shape, dt=F32):
        t = self.nc.dram_tensor(name, list(shape), dt, kind="ExternalOutput")
        self.ext_out[name] = t
        return t.ap()

    def dram(self, name, shape, dt, shared=False):
        if shared:
            return self.nc.dram_tensor(name, list(shape), dt,
                                       addr_space="Shared").ap()
        return self.nc.dram_tensor(name, list(shape), dt).ap()

    def _uniq(self, name):
        self._uid = getattr(self, "_uid", 0) + 1
        return f"{name}_u{self._uid}"

    def sb(self, name, shape, dt, st=None, side=None):
        return (st or self._root).enter_context(
            self.nc.sbuf_tensor(self._uniq(name), list(shape), dt, side=side))

    def ps(self, name, shape, dt=F32, st=None):
        return (st or self._root).enter_context(
            self.nc.psum_tensor(self._uniq(name), list(shape), dt))

    def sem(self, name):
        return self._root.enter_context(self.nc.semaphore(name))

    def tap(self, name, src, shape, dt, deps):
        if name not in self.taps:
            return
        ap = self.outp("tap_" + name, shape, dt)
        self.qs.go("tap", ap[:, :], src, deps=deps, fresh=False)

    def barrier(self, dma=True):
        toks = [self.v.tok(), self.a.tok(), self.p.tok(), self.pe.tok()]
        if dma:
            for q in (self.qs, self.qg):
                for name, (sem, n) in q.slots.items():
                    if n:
                        toks.append((sem, n))
        for eng in (self.v, self.a, self.p, self.pe, self.qs, self.qg):
            for t in toks:
                eng.dep(t)
        return toks

    def build(self):
        nc = self.nc
        self.v = Chain(nc.vector, self.sem("c_v"))
        self.a = Chain(nc.scalar, self.sem("c_a"))
        self.p = Chain(nc.gpsimd, self.sem("c_p"))
        self.pe = Pe(nc.tensor, self.sem("c_pe"))
        self.qs = Dma(self, nc.sync, "ds")
        self.qg = Dma(self, nc.gpsimd, "dg")

        self.declare_io()
        self.init_consts()
        sa = self.stop_after
        self.s0_layernorm()
        self.barrier()
        self._ss_stack.close()
        s1cuts = ("gm", "s1", "s1a", "s1b", "s1c", "s1d", "s1e")
        if sa != "s0":
            self.ivt_tok = self.global_max(self.inv2)
            self.barrier()
        if sa not in ("s0", "gm"):
            self.s1_ffn()
            self.barrier()
        if sa not in (("s0",) + s1cuts):
            self.s2_inproj_conv()
            self.barrier()
        self._rms_stack.close()
        self._act_stack.close()
        if sa not in (("s0", "s2") + s1cuts):
            self.s3a_ssm_local()
            self.barrier()
        if sa not in (("s0", "s2", "s3a") + s1cuts):
            self.s3b_ssm_corr()
            self.barrier()
        self._mid_stack.close()
        if sa not in (("s0", "s2", "s3a", "s3b") + s1cuts):
            self.s4_gate()
            self.barrier()
        if sa not in (("s0", "s2", "s3a", "s3b", "s4") + s1cuts):
            self.s5_outproj()
        self.finish()
        return self

    def declare_io(self):
        TOK, T, NCH = self.TOK, self.T, self.NCH
        self.x = self.inp("x", [D, T])
        self.win = self.inp("win", [NOC, 128, DC * 128])
        self.wffn = self.inp("wffn", [DC, 128, DC * 128])
        self.wop = self.inp("wop", [DC, 128, DIN])
        self.alora = self.inp("alora", [128, DC * 32])
        self.bmat = self.inp("bmat", [32, D])
        self.limes = self.inp("limes", [8, 32])
        self.limest = self.inp("limest", [32, 8])
        self.nw = self.inp("nw", [128, DC])
        self.gnw = self.inp("gnw", [128, 64])
        self.cw = self.inp("cw", [128, 320])
        self.cb = self.inp("cb", [128, 80])
        self.dtb = self.inp("dtb", [128, 1])
        self.alog = self.inp("alog", [128, 1])
        self.dpx = self.inp("dpx", [128, 64])
        self.bff = self.inp("bff", [1, D])
        self.bcol = self.inp("bcol", [128, DC])
        self.pmask = self.inp("pmask", [128, 4])
        self.NAA = self.naa if self.naa else 8192
        self.xa = self.inp("xa", [D, self.NAA])
        self.w8a = self.inp("w8a", [128, DC * 40])
        self.out = self.outp("out", [D, TOK])

        self.h_dram = self.dram("h_dram", [D, T], BF16)
        self.hbc_dram = self.dram("hbc_dram", [CONVD, self.TOKA], BF16)
        self.H_dram = self.dram("H_dram", [D, TOK], BF16)
        self.yfm_dram = self.dram("yfm_dram", [DIN, TOK], BF16)
        self.ygn_dram = self.dram("ygn_dram", [DIN, TOK], BF16)
        self.acsflat_dram = self.dram("acsflat_dram", [self.NCHA, HM * CH], F32)
        self.ea_dram = self.dram("ea_dram", [self.NCHA, HM], BF16)
        self.mx_dram = self.dram("mx_dram", [40, 2], F32)
        self.rsq_dram = self.dram("rsq_dram", [8, TOK], F32)

        self._act_stack = ExitStack()
        self._rms_stack = ExitStack()
        self._ss_stack = ExitStack()
        self._mid_stack = ExitStack()
        self.act = self.sb("act", [128, DC * T], BF16, st=self._act_stack,
                           side="right")
        self.rmsrow = self.sb("rmsrow", [1, T], F32, st=self._rms_stack, side="right")
        self.ssrow = self.sb("ssrow", [1, T], F32, st=self._ss_stack, side="right")
        self.s1row = self.sb("s1row", [1, T], F32)
        self.msrow = self.sb("msrow", [1, T], F32)
        self.zp = self.sb("zp", [32, TOK], BF16)
        self.inv2 = self.sb("inv2", [128, 2], F32)
        self.ident_b = self.sb("ident_b", [128, 128], BF16)
        self.ident_f = self.sb("ident_f", [128, 128], F32)
        self.ones_f = self.sb("ones_f", [128, 128], F32)
        self.ones_b = self.sb("ones_b", [1, 512], BF16)
        self.upmask_b = self.sb("upmask_b", [128, 128], BF16)
        self.up8 = self.sb("up8", [8, 8], F32)
        self.limes_b = self.sb("limes_b", [8, 32], BF16)
        self.limest_b = self.sb("limest_b", [32, 8], BF16)
        self.nw_s = self.sb("nw_s", [128, DC], F32)
        self.gnw_s = self.sb("gnw_s", [128, 64], F32)
        self.cw_s = self.sb("cw_s", [128, 320], F32)
        self.cb_s = self.sb("cb_s", [128, 80], F32)
        self.dtb_s = self.sb("dtb_s", [128, 1], F32)
        self.alog_s = self.sb("alog_s", [128, 3], F32)
        self.dpx_s = self.sb("dpx_s", [128, 64], F32)
        self.bcol_s = self.sb("bcol_s", [128, DC], F32)
        self.konst = self.sb("konst", [128, 2], F32)
        self.pmask_s = self.sb("pmask_s", [128, 4], F32)

        self.mb = banks_of(TOK)
        self.mba = banks_of(self.TOKA)
        self.sbk = banks_of(T)

    def init_consts(self):
        p, qld, qw = self.p, self.qs, self.qg
        t0 = p.op(lambda e: e.memset(self.ident_b[:, :], 0.0))
        t1 = p.op(lambda e: e.memset(self.ident_f[:, :], 0.0))
        t2 = p.op(lambda e: e.memset(self.upmask_b[:, :], 0.0))
        t3 = p.op(lambda e: e.memset(self.up8[:, :], 1.0))
        p.op(lambda e: e.memset(self.ones_f[:, :], 1.0))
        p.op(lambda e: e.memset(self.ones_b[:, :], 1.0))
        p.op(lambda e: e.affine_select(out=self.ident_b[:, :], in_=self.ident_b[:, :],
                                       compare_op=ALU.not_equal, fill=1.0, base=0,
                                       pattern=[[-1, 128]], channel_multiplier=1), deps=[t0])
        p.op(lambda e: e.affine_select(out=self.ident_f[:, :], in_=self.ident_f[:, :],
                                       compare_op=ALU.not_equal, fill=1.0, base=0,
                                       pattern=[[-1, 128]], channel_multiplier=1), deps=[t1])
        p.op(lambda e: e.affine_select(out=self.upmask_b[:, :], in_=self.upmask_b[:, :],
                                       compare_op=ALU.is_gt, fill=1.0, base=0,
                                       pattern=[[-1, 128]], channel_multiplier=1), deps=[t2])
        p.op(lambda e: e.affine_select(out=self.up8[:, :], in_=self.up8[:, :],
                                       compare_op=ALU.is_gt, fill=0.0, base=0,
                                       pattern=[[-1, 8]], channel_multiplier=1), deps=[t3])
        p.op(lambda e: e.memset(self.konst[:, 0:1], EPS))
        p.op(lambda e: e.memset(self.konst[:, 1:2], 1.0))
        self.cinit = p.tok()
        qw.go("wc", self.limes_b[:, :], self.limes[:, :], fresh=False)
        qw.go("wc", self.limest_b[:, :], self.limest[:, :], fresh=False)
        self.wconst = qw.tok("wc")
        for dst, src in [(self.nw_s[:, :], self.nw), (self.gnw_s[:, :], self.gnw),
                         (self.cw_s[:, :], self.cw), (self.cb_s[:, :], self.cb),
                         (self.dtb_s[:, :], self.dtb), (self.alog_s[:, 0:1], self.alog),
                         (self.dpx_s[:, :], self.dpx),
                         (self.pmask_s[:, :], self.pmask),
                         (self.bcol_s[:, :], self.bcol)]:
            qld.go("prm", dst, src[:, :], fresh=False)
        self.pconst = qld.tok("prm")

    def s0_layernorm(self):
        TOK, T = self.TOK, self.T
        v, a, p, pe, qld = self.v, self.a, self.p, self.pe, self.qs
        with ExitStack() as st:
            xbuf = [self.sb(f"s0x{i}", [128, 512], F32, st) for i in range(2)]
            sqb = [self.sb(f"s0q{i}", [128, 512], F32, st) for i in range(2)]
            srow = self.sb("srow", [1, 512], F32, st)
            qrow = self.sb("qrow", [1, 512], F32, st)
            mrow = self.sb("mrow", [1, 512], F32, st)
            p_sum = self.ps("p_sum", [1, 512], st=st)
            p_sq = self.ps("p_sq", [1, 512], st=st)
            dctok = {}
            ev_tok = None
            kk = 0
            for (s0, w) in self.sbk:
                for dc in range(DC):
                    i = dc % 2
                    xt = qld.go(f"s0x{i}", xbuf[i][:, :w],
                                self.x[dc * 128:(dc + 1) * 128, s0:s0 + w],
                                deps=[dctok.get(kk - 2)])
                    sq = a.op(lambda e, i=i, w=w: e.activation(
                        sqb[i][:, :w], xbuf[i][:, :w], AF.Square),
                        deps=[xt, dctok.get(kk - 2)])
                    pe.mm(p_sum[:, :w], self.ones_f[:, 0:1], xbuf[i][:, :w],
                          start=(dc == 0), stop=(dc == DC - 1),
                          deps=[xt, ev_tok, self.cinit])
                    pe.mm(p_sq[:, :w], self.ones_f[:, 0:1], sqb[i][:, :w],
                          start=(dc == 0), stop=(dc == DC - 1), deps=[sq])
                    dctok[kk] = pe.mark()
                    kk += 1
                mm_tok = pe.tok()
                sl = slice(s0, s0 + w)
                v.op(lambda e, w=w: e.tensor_scalar(mrow[:, :w], p_sum[:, :w],
                                                    1.0 / D, None, ALU.mult),
                     deps=[mm_tok])
                v.op(lambda e, w=w: e.tensor_scalar(qrow[:, :w], p_sq[:, :w],
                                                    1.0 / D, None, ALU.mult))
                v.op(lambda e, w=w: e.tensor_mul(srow[:, :w], mrow[:, :w], mrow[:, :w]))
                ev_tok = v.op(lambda e, w=w: e.tensor_sub(qrow[:, :w], qrow[:, :w],
                                                          srow[:, :w]))
                a.op(lambda e, w=w: e.activation(srow[:, :w], qrow[:, :w], AF.Ln,
                                                 bias=self.konst[0:1, 0:1]),
                     deps=[ev_tok])
                sst = a.op(lambda e, sl=sl, w=w: e.activation(
                    self.ssrow[:, sl], srow[:, :w], AF.Exp, scale=-0.5))
                v.op(lambda e, sl=sl, w=w: e.tensor_scalar(
                    self.s1row[:, sl], self.ssrow[:, sl], 1.0, None, ALU.add),
                    deps=[sst])
                v.op(lambda e, sl=sl, w=w: e.tensor_mul(
                    self.msrow[:, sl], mrow[:, :w], self.ssrow[:, sl]))
                v.op(lambda e, sl=sl, w=w: e.tensor_mul(
                    srow[:, :w], qrow[:, :w], self.ssrow[:, sl]))
                tv = v.op(lambda e, sl=sl, w=w: e.tensor_mul(
                    srow[:, :w], srow[:, :w], self.ssrow[:, sl]))
                a.op(lambda e, w=w: e.activation(srow[:, :w], srow[:, :w], AF.Ln,
                                                 bias=self.konst[0:1, 0:1]), deps=[tv])
                a.op(lambda e, sl=sl, w=w: e.activation(
                    self.rmsrow[:, sl], srow[:, :w], AF.Exp, scale=-0.5))
            self.stats_tok = [v.tok(), a.tok()]
        self.barrier()

        with ExitStack() as st:
            s_rep = self.sb("s_rep0", [128, 512], F32, st)
            ms_rep = self.sb("ms_rep0", [128, 512], F32, st)
            xw = [self.sb(f"s0w{i}", [128, 512], F32, st) for i in range(2)]
            xsc = [self.sb(f"s0s{i}", [128, 512], F32, st) for i in range(2)]
            p_rep = self.ps("p_rep", [128, 512], st=st)
            prev = [None, None]
            for (s0, w) in self.sbk:
                rt = self._replicate_rows(
                    p_rep, [(self.ssrow, s_rep), (self.msrow, ms_rep)], [(0, w)],
                    deps=self.stats_tok + [self.v.tok(), self.p.tok()], off=s0)
                for dc in range(DC):
                    i = dc % 2
                    eng = v if i == 0 else p
                    xt = qld.go(f"s0w{i}", xw[i][:, :w],
                                self.x[dc * 128:(dc + 1) * 128, s0:s0 + w],
                                deps=[prev[i]])
                    eng.op(lambda e, i=i, w=w: e.tensor_mul(
                        xsc[i][:, :w], xw[i][:, :w], s_rep[:, :w]), deps=[xt, rt])
                    tk = eng.op(lambda e, i=i, dc=dc, s0=s0, w=w: e.tensor_sub(
                        self.act[:, dc * T + s0:dc * T + s0 + w], xsc[i][:, :w],
                        ms_rep[:, :w]))
                    prev[i] = tk
            self.xn_done = [self.v.tok(), self.p.tok()]
        self.tap("xn", self.act[:, :], [128, DC * self.T], BF16, self.xn_done)

    def _replicate_rows(self, p_rep, pairs, bks, deps=(), off=0):
        v, pe = self.v, self.pe
        ev = None
        for (row, dst) in pairs:
            for (s0, w) in bks:
                pe.mm(p_rep[:, :w], self.ones_f[0:1, :], row[:, off + s0:off + s0 + w],
                      start=True, stop=True, deps=list(deps) + [ev, self.cinit])
                tk = pe.mark()
                ev = v.op(lambda e, s0=s0, w=w, dst=dst: e.tensor_copy(
                    dst[:, s0:s0 + w], p_rep[:, :w]), deps=[tk])
        return ev

    def s1_ffn(self):
        TOK, T = self.TOK, self.T
        v, a, p, pe = self.v, self.a, self.p, self.pe
        qld, qst, qw = self.qs, self.qs, self.qg
        NA = TOK // 2
        with ExitStack() as st:
            zsb = self.zsb = self.sb("zsb", [32, TOK], BF16, st)
            self.alora_b = self.sb("alora_b", [128, DC * 32], BF16, st)
            self.qg.go("wc", self.alora_b[:, :], self.alora[:, :], fresh=False)
            self.wconst = self.qg.tok("wc")
            Hsl = self.sb("Hsl", [8, NA], BF16, st)
            dsl = self.sb("dsl", [8, NA], BF16, st)
            zanch = self.sb("zanch", [32, NA], BF16, st)
            inv2 = self.inv2
            wfm = self.sb("wfm", [8, NA], BF16, st)
            st_mm = st.enter_context(ExitStack())
            wbuf = [self.sb(f"s1w{i}", [128, DC * 128], BF16, st_mm) for i in range(2)]
            Hbuf = [self.sb(f"s1h{i}", [128, 512], BF16, st_mm) for i in range(2)]
            p_h = [self.ps(f"p_h{i}", [128, 512], st=st) for i in range(2)]
            wtok = [None, None]
            guard = [[], []]
            hst = [None, None]
            hs_done = []

            wtok[0] = qw.go("w0", wbuf[0][:, :], self.wffn[0])

            def ffn_oc(oc):
                i = oc % 2
                if oc + 1 < DC:
                    wtok[(oc + 1) % 2] = qw.go(f"w{(oc + 1) % 2}", wbuf[(oc + 1) % 2][:, :],
                                               self.wffn[oc + 1], deps=[pe.tok()])
                for j, (s0, w) in enumerate(self.mb):
                    pj = p_h[j % 2]
                    u = j % 2
                    for k in range(DC):
                        pe.mm(pj[:, :w], wbuf[i][:, k * 128:(k + 1) * 128],
                              self.act[:, k * T + self.OWN0 + s0:
                                        k * T + self.OWN0 + s0 + w],
                              start=(k == 0), stop=(k == DC - 1),
                              deps=[wtok[i]] + guard[u] + (self.xn_done if k == 0 else []))
                    mt = pe.mark()
                    atk = a.op(lambda e, u=u, w=w, pj=pj: e.activation(
                        Hbuf[u][:, :w], pj[:, :w], AF.Copy), deps=[mt, hst[u]])
                    guard[u] = [atk]
                    if oc == 0 and "noHsl" not in self.dbg:
                        vtk = v.op(lambda e, s0=s0, w=w, pj=pj: e.tensor_scalar(
                            Hsl[:, s0 // 2:(s0 + w) // 2],
                            pj[0:8, 1:w:2], self.bcol_s[0:8, 0:1], None, ALU.add),
                            deps=[mt, atk, self.pconst])
                        guard[u].append(vtk)
                    if "nospill" not in self.dbg:
                        stt = qst.go(f"hs{u}", self.H_dram[oc * 128:(oc + 1) * 128,
                                                          s0:s0 + w],
                                     Hbuf[u][:, :w], deps=guard[u])
                        hst[u] = stt
                        hs_done.append(stt)

            ffn_oc(0)
            if self.stop_after == "s1a":
                return
            st_z = st.enter_context(ExitStack())
            p_z = self.ps("p_z", [32, 512], st=st_z)
            z_ev = None
            for (s0, w) in self.mb:
                for k in range(DC):
                    pe.mm(p_z[:, :w], self.alora_b[:, k * 32:(k + 1) * 32],
                          self.act[:, k * T + self.OWN0 + s0:
                                    k * T + self.OWN0 + s0 + w],
                          start=(k == 0), stop=(k == DC - 1),
                          deps=[z_ev, self.wconst])
                mt = pe.mark()
                z_ev = v.op(lambda e, s0=s0, w=w: e.tensor_copy(self.zsb[:, s0:s0 + w],
                                                                p_z[:, :w]), deps=[mt])
            za = v.op(lambda e: e.tensor_copy(zanch[:, :], self.zsb[:, 1:TOK:2]),
                      deps=[z_ev])
            p_dsl = self.ps("p_dsl", [8, 512], st=st_z)
            dsl_ev = None
            for (s0, w) in banks_of(NA):
                pe.mm(p_dsl[:, :w], self.limest_b[:, :], zanch[:, s0:s0 + w],
                      start=True, stop=True, deps=[za, self.wconst, dsl_ev])
                mt = pe.mark()
                dsl_ev = v.op(lambda e, s0=s0, w=w: e.tensor_copy(dsl[:, s0:s0 + w],
                                                                  p_dsl[:, :w]), deps=[mt])
            if self.stop_after == "s1b":
                return
            hsl_last = dsl_ev
            self.barrier(dma=False)
            st_z.close()
            ivt = self.ivt_tok
            if self.stop_after == "s1c":
                return
            for oc in range(1, DC):
                ffn_oc(oc)
            self.s1_h_done = list(hs_done)
            self.barrier(dma=False)
            st_mm.close()
            if self.stop_after == "s1d":
                return
            p_t16 = self.ps("p_t16", [128, 16], BF16, st=st)
            p_wn = self.ps("p_wn", [8, 128], BF16, st=st)
            tmh = self.sb("tmh", [128, 16], F32, st)
            g1 = self.sb("g1", [128, 8], F32, st)
            g2 = self.sb("g2", [128, 8], F32, st)
            gcol = self.sb("gcol", [128, 4], F32, st)
            wn_b = self.sb("wn_b", [128, 8], BF16, st)
            ev1 = ev2 = None
            for at in range(NA // 128):
                sl = slice(at * 128, (at + 1) * 128)
                pe.tr(p_t16[:, 0:8], Hsl[:, sl], self.ident_b[0:8, 0:8],
                      deps=[hsl_last, ev1, self.cinit])
                pe.tr(p_t16[:, 8:16], dsl[:, sl], self.ident_b[0:8, 0:8])
                tt = pe.mark()
                ev1 = v.op(lambda e: e.tensor_copy(tmh[:, :], p_t16[:, :]), deps=[tt])
                v.op(lambda e: e.tensor_scalar(g1[:, :], tmh[:, 8:16], inv2[:, 1:2], None,
                                               ALU.mult), deps=[ivt])
                v.op(lambda e: e.scalar_tensor_tensor(g1[:, :], tmh[:, 0:8], inv2[:, 0:1],
                                                      g1[:, :], ALU.mult, ALU.add))
                v.op(lambda e: e.reduce_max(gcol[:, 0:1], g1[:, :], axis=AXX))
                v.op(lambda e: e.tensor_scalar(gcol[:, 1:2], gcol[:, 0:1], -1.0, None,
                                               ALU.mult))
                a.op(lambda e: e.activation(g2[:, :], g1[:, :], AF.Exp, bias=gcol[:, 1:2],
                                            accum_out=gcol[:, 2:3]), deps=[v.tok()])
                v.op(lambda e: e.reciprocal(gcol[:, 3:4], gcol[:, 2:3]), deps=[a.tok()])
                v.op(lambda e: e.tensor_scalar(g1[:, :], g2[:, :], gcol[:, 3:4], None,
                                               ALU.mult))
                v.op(lambda e: e.max(out=g2[:, :], in_=g1[:, :]))
                v.op(lambda e: e.tensor_copy(gcol[:, 1:2], g2[:, 1:2]))
                v.op(lambda e: e.tensor_scalar(g2[:, :], g1[:, :], gcol[:, 1:2], -2.0,
                                               ALU.subtract, ALU.mult))
                a.op(lambda e: e.activation(g2[:, :], g2[:, :], AF.Exp), deps=[v.tok()])
                v.op(lambda e: e.tensor_scalar(g2[:, :], g2[:, :], 1.0, None, ALU.add),
                     deps=[a.tok()])
                v.op(lambda e: e.reciprocal(g2[:, :], g2[:, :]))
                v.op(lambda e: e.tensor_mul(g2[:, :], g1[:, :], g2[:, :]))
                v.op(lambda e: e.reduce_sum(gcol[:, 0:1], g2[:, :], axis=AXX))
                v.op(lambda e: e.tensor_scalar(gcol[:, 0:1], gcol[:, 0:1], 1e-9, None,
                                               ALU.add))
                v.op(lambda e: e.reciprocal(gcol[:, 1:2], gcol[:, 0:1]))
                wnt = v.op(lambda e: e.tensor_scalar(wn_b[:, :], g2[:, :], gcol[:, 1:2],
                                                     None, ALU.mult))
                pe.tr(p_wn[:, :], wn_b[:, :], self.ident_b[:, :], deps=[wnt, ev2])
                tt = pe.mark()
                ev2 = v.op(lambda e, at=at: e.tensor_copy(
                    wfm[:, at * 128:(at + 1) * 128], p_wn[:, :]), deps=[tt])
            if self.stop_after == "s1e":
                return
            wf2 = self.sb("wf2", [8, TOK], BF16, st)
            v.op(lambda e: e.tensor_copy(wf2[:, 0:TOK:2], wfm[:, :]), deps=[ev2])
            wft = v.op(lambda e: e.tensor_copy(wf2[:, 1:TOK:2], wfm[:, :]))
            p_pm = self.ps("p_pm", [32, 512], st=st)
            zp_ev = None
            for (s0, w) in self.mb:
                pe.mm(p_pm[:, :w], self.limes_b[:, :], wf2[:, s0:s0 + w],
                      start=True, stop=True, deps=[wft, zp_ev])
                mt = pe.mark()
                zp_ev = v.op(lambda e, s0=s0, w=w: e.scalar_tensor_tensor(
                    self.zp[:, s0:s0 + w], self.zsb[:, s0:s0 + w], 0.5, p_pm[:, :w],
                    ALU.mult, ALU.mult), deps=[mt])
            self.zp_done = zp_ev
        self.tap("zp", self.zp[:, :], [32, self.TOK], BF16, [self.zp_done])
        self.tap("H", self.H_dram[:, :], [D, self.TOK], BF16, self.s1_h_done)

    def global_max(self, inv2):
        v, a, p, pe = self.v, self.a, self.p, self.pe
        qld, qw = self.qs, self.qg
        NAA = self.NAA
        with ExitStack() as s2:
            w8ab = self.sb("w8ab", [128, DC * 40], BF16, s2)
            onesb2 = self.sb("onesb2", [128, 1], BF16, s2)
            xab = [self.sb(f"gmx{i}", [128, 512], BF16, s2) for i in range(2)]
            sqb2 = self.sb("gmsq", [128, 512], BF16, s2)
            zb = self.sb("gmzb", [32, 512], BF16, s2)
            m2r = self.sb("gm_m", [1, 512], F32, s2)
            s2r = self.sb("gm_s", [1, 512], F32, s2)
            t1r = self.sb("gm_t", [1, 512], F32, s2)
            w1c = self.sb("gm_w1", [32, 2], F32, s2)
            rep32 = self.sb("gm_rep", [32, 512], BF16, s2)
            srep = self.sb("gm_srep", [32, 512], BF16, s2)
            t32 = self.sb("gm_t32", [32, 512], F32, s2)
            t8 = t32
            hx = self.sb("gm_hx", [8, 2], F32, s2)
            dx = self.sb("gm_dx", [8, 2], F32, s2)
            mxf = self.sb("gm_mxf", [1, 32], F32, s2)
            mx2 = self.sb("gm_mx2", [1, 2], F32, s2)
            maxrep = self.sb("maxrep", [128, 2], F32, s2)
            p_s = self.ps("gp_s", [1, 512], st=s2)
            p_q = self.ps("gp_q", [1, 512], st=s2)
            p_u8 = self.ps("gp_u8", [8, 512], st=s2)
            p_v = self.ps("gp_v", [32, 512], st=s2)
            p_r = self.ps("gp_r", [32, 512], st=s2)
            p_d = self.ps("gp_d", [8, 512], st=s2)
            wt = qw.go("gmw", w8ab[:, :], self.w8a[:, :])
            ot = p.op(lambda e: e.memset(onesb2[:, :], 1.0))
            zz = p.op(lambda e: e.memset(hx[:, :], 0.0))
            zz2 = p.op(lambda e: e.memset(dx[:, :], 0.0))
            for dc in range(DC):
                pe.mm(p_u8[:, 0:1], w8ab[:, dc * 40:dc * 40 + 8], onesb2[:, :],
                      start=(dc == 0), stop=(dc == DC - 1), deps=[wt, ot])
                pe.mm(p_v[:, 0:1], w8ab[:, dc * 40 + 8:dc * 40 + 40], onesb2[:, :],
                      start=(dc == 0), stop=(dc == DC - 1))
            wmk = pe.mark()
            v.op(lambda e: e.tensor_copy(w1c[0:8, 0:1], p_u8[:, 0:1]), deps=[wmk])
            w1t = v.op(lambda e: e.tensor_copy(w1c[:, 1:2], p_v[:, 0:1]))
            ev = w1t
            for (s0, w) in banks_of(NAA):
                sl = slice(s0, s0 + w)
                for dc in range(DC):
                    i = dc % 2
                    xt = qw.go(f"gmx{i}", xab[i][:, :w],
                               self.xa[dc * 128:(dc + 1) * 128, sl],
                               deps=[pe.tok(), v.tok()])
                    sq = v.op(lambda e, i=i, w=w: e.tensor_mul(
                        sqb2[:, :w], xab[i][:, :w], xab[i][:, :w]),
                        deps=[xt, pe.tok()])
                    pe.mm(p_s[:, :w], onesb2[:, :], xab[i][:, :w],
                          start=(dc == 0), stop=(dc == DC - 1),
                          deps=[xt, ev, ot, self.cinit])
                    pe.mm(p_q[:, :w], onesb2[:, :], sqb2[:, :w],
                          start=(dc == 0), stop=(dc == DC - 1), deps=[sq])
                    pe.mm(p_u8[:, :w], w8ab[:, dc * 40:dc * 40 + 8], xab[i][:, :w],
                          start=(dc == 0), stop=(dc == DC - 1))
                    pe.mm(p_v[:, :w], w8ab[:, dc * 40 + 8:dc * 40 + 40],
                          xab[i][:, :w],
                          start=(dc == 0), stop=(dc == DC - 1))
                    pe.mark()
                mk = pe.tok()
                v.op(lambda e, w=w: e.tensor_scalar(m2r[:, :w], p_s[:, :w],
                                                    1.0 / D, None, ALU.mult),
                     deps=[mk])
                v.op(lambda e, w=w: e.tensor_scalar(t1r[:, :w], p_q[:, :w],
                                                    1.0 / D, None, ALU.mult))
                v.op(lambda e, w=w: e.tensor_mul(s2r[:, :w], m2r[:, :w], m2r[:, :w]))
                v.op(lambda e, w=w: e.tensor_sub(t1r[:, :w], t1r[:, :w], s2r[:, :w]))
                a.op(lambda e, w=w: e.activation(t1r[:, :w], t1r[:, :w], AF.Ln,
                                                 bias=self.konst[0:1, 0:1]),
                     deps=[v.tok()])
                a.op(lambda e, w=w: e.activation(s2r[:, :w], t1r[:, :w], AF.Exp,
                                                 scale=-0.5))
                pe.mm(p_r[:, :w], self.ones_f[0:1, 0:32], m2r[:, :w], start=True,
                      stop=True, deps=[a.tok(), ev])
                rmk = pe.mark()
                v.op(lambda e, w=w: e.tensor_copy(rep32[:, :w], p_r[:, :w]),
                     deps=[rmk])
                pe.mm(p_r[:, :w], self.ones_f[0:1, 0:32], s2r[:, :w], start=True,
                      stop=True, deps=[v.tok()])
                rmk = pe.mark()
                v.op(lambda e, w=w: e.tensor_copy(srep[:, :w], p_r[:, :w]),
                     deps=[rmk])
                v.op(lambda e, w=w: e.scalar_tensor_tensor(
                    t8[0:8, :w], rep32[0:8, :w], w1c[0:8, 0:1], p_u8[:, :w],
                    ALU.mult, ALU.subtract))
                v.op(lambda e, w=w: e.tensor_mul(t8[0:8, :w], t8[0:8, :w],
                                                 srep[0:8, :w]))
                v.op(lambda e, w=w: e.tensor_scalar(t8[0:8, :w], t8[0:8, :w],
                                                    self.bcol_s[0:8, 0:1], None,
                                                    ALU.subtract),
                     deps=[self.pconst])
                v.op(lambda e, w=w: e.scalar_tensor_tensor(
                    t32[:, :w], rep32[:, :w], w1c[:, 1:2], p_v[:, :w],
                    ALU.mult, ALU.subtract))
                v.op(lambda e, w=w: e.tensor_mul(t32[:, :w], t32[:, :w],
                                                 srep[:, :w]))
                zbt = v.op(lambda e, w=w: e.tensor_copy(zb[:, :w], t32[:, :w]))
                v.op(lambda e, w=w: e.reduce_max(hx[:, 1:2], t8[0:8, :w], axis=AXX,
                                                 apply_absolute_value=True),
                     deps=[zz])
                v.op(lambda e: e.tensor_max(hx[:, 0:1], hx[:, 0:1], hx[:, 1:2]))
                pe.mm(p_d[:, :w], self.limest_b[:, :], zb[:, :w], start=True,
                      stop=True, deps=[zbt, self.wconst, ev])
                dmk = pe.mark()
                v.op(lambda e, w=w: e.reduce_max(dx[:, 1:2], p_d[:, :w], axis=AXX,
                                                 apply_absolute_value=True),
                     deps=[dmk, zz2])
                ev = v.op(lambda e: e.tensor_max(dx[:, 0:1], dx[:, 0:1], dx[:, 1:2]))
            bt = qld.go("mx", self.mx_dram[0:8, :], hx[:, :], deps=[ev])
            bt = qld.go("mx", self.mx_dram[8:16, :], dx[:, :], deps=[ev],
                        fresh=False)
            bt2 = qld.go("mx", mxf[:, :],
                         self.mx_dram.rearrange("a b -> (a b)").unsqueeze(0)[:, 0:32],
                         deps=[bt])
            v.op(lambda e: e.reduce_max(mx2[:, 0:1], mxf[0:1, 0:16:2], axis=AXX),
                 deps=[bt2])
            v.op(lambda e: e.reduce_max(mx2[:, 1:2], mxf[0:1, 16:32:2], axis=AXX))
            m2t = v.op(lambda e: e.tensor_scalar(mx2[:, :], mx2[:, :], 1e-6, None,
                                                 ALU.max))
            bt3 = qld.go("mx", self.mx_dram[16:17, 0:2], mx2[:, :], deps=[m2t])
            gt = qld.go("mx", maxrep[:, :],
                        self.mx_dram[16:17, 0:2].broadcast_to([128, 2]), deps=[bt3])
            v.op(lambda e: e.reciprocal(inv2[:, :], maxrep[:, :]), deps=[gt])
            ivt = v.op(lambda e: e.tensor_scalar(inv2[:, :], inv2[:, :], 0.5, None,
                                                 ALU.mult))
        return ivt

    def s2_inproj_conv(self):
        TOK, T, NCH = self.TOK, self.T, self.NCH
        v, a, p, pe = self.v, self.a, self.p, self.pe
        qld, qst, qw = self.qs, self.qs, self.qg
        TOKA, NCHA = self.TOKA, self.NCHA
        self.dtsp = self.sb("dtsp", [128, TOKA], F32, st=self._mid_stack)
        self.acs = self.sb("acs", [128, TOKA], F32, st=self._mid_stack)
        self.csum = self.sb("csum", [128, NCHA], F32, st=self._mid_stack)
        with ExitStack() as st:
            rms_rep = self.sb("rms_rep", [128, T], F32, st)
            p_rep2 = self.ps("p_rep2", [128, 512], st=st)
            rept = self._replicate_rows(p_rep2, [(self.rmsrow, rms_rep)], self.sbk)
            hb = [self.sb(f"s2hb{i}", [128, T], BF16, st) for i in range(2)]
            htok = {}
            for dc in range(DC):
                i = dc % 2
                eng = v if i == 0 else p
                eng.op(lambda e, i=i, dc=dc: e.tensor_mul(
                    hb[i][:, :], self.act[:, dc * T:(dc + 1) * T], rms_rep[:, :]),
                    deps=[rept, self.pconst])
                tk = eng.op(lambda e, i=i, dc=dc: e.tensor_scalar(
                    self.act[:, dc * T:(dc + 1) * T], hb[i][:, :],
                    self.nw_s[:, dc:dc + 1], None, ALU.mult))
                htok[dc] = tk
                qst.go(f"hsp{i}", self.h_dram[dc * 128:(dc + 1) * 128, :],
                       self.act[:, dc * T:(dc + 1) * T], deps=[tk])
            self.h_spilled = [self.qs.tok("hsp0"), self.qs.tok("hsp1")]
        self.tap("h", self.act[:, :], [128, DC * T], BF16, list(htok.values()))
        self.barrier(dma=False)
        self._rms_stack.close()
        self._rms_stack = ExitStack()

        with ExitStack() as st:
            ea_b = self.sb("ea_b", [128, self.NCHA], BF16, st)
            st_conv = st.enter_context(ExitStack())
            wbuf = [self.sb(f"s2w{i}", [128, DC * 128], BF16, st_conv) for i in range(2)]
            xr = [self.sb(f"s2xr{i}", [128, T], BF16, st_conv) for i in range(2)]
            hbct = [self.sb(f"s2hc{i}", [128, 512], BF16, st_conv) for i in range(2)]
            diag = self.sb("s2diag", [128, 4 * 128], BF16, st_conv)
            dtraw = self.dtsp
            p_m = [self.ps(f"p_m{i}", [128, 512], st=st) for i in range(2)]
            p_bnd = self.ps("p_bnd", [128, 512], st=st)
            p_c = [self.ps(f"p_c{i}", [128, 512], st=st) for i in range(2)]

            wtok = [None, None]
            mguard = [None, None]
            bguard = None
            cguard = [None, None]
            xr_free = [None, None]
            hb_free = [None, None]
            dg_free = None
            ocs = list(range(OC_HBC0, OC_DT + 1))
            wtok[ocs[0] % 2] = qw.go(f"w{ocs[0] % 2}", wbuf[ocs[0] % 2][:, :],
                                     self.win[ocs[0]])
            hbc_done = []
            for oi, oc in enumerate(ocs):
                i = oc % 2
                q = oc - OC_HBC0
                is_dt = (oc == OC_DT)
                if oi + 1 < len(ocs):
                    nxt = ocs[oi + 1]
                    wtok[nxt % 2] = qw.go(f"w{nxt % 2}", wbuf[nxt % 2][:, :],
                                          self.win[nxt], deps=[pe.tok()])
                if not is_dt:
                    for k in range(DC):
                        pe.mm(p_bnd[:, 0:NB], wbuf[i][:, k * 128:(k + 1) * 128],
                              self.act[:, k * T: k * T + NB],
                              start=(k == 0), stop=(k == DC - 1),
                              deps=[wtok[i], bguard] + ([htok[k]] if k in htok else []))
                    bmk = pe.mark()
                    bguard = v.op(lambda e, i=i: e.tensor_copy(xr[i][:, 0:NB],
                                                               p_bnd[:, 0:NB]),
                                  deps=[bmk, xr_free[i]])
                for j, (s0, w) in enumerate(self.mba):
                    pj = p_m[j % 2]
                    for k in range(DC):
                        pe.mm(pj[:, :w], wbuf[i][:, k * 128:(k + 1) * 128],
                              self.act[:, k * T + NB + s0: k * T + NB + s0 + w],
                              start=(k == 0), stop=(k == DC - 1),
                              deps=[wtok[i], mguard[j % 2], htok[k]])
                    mk = pe.mark()
                    if is_dt:
                        mguard[j % 2] = v.op(lambda e, s0=s0, w=w, pj=pj: e.tensor_copy(
                            dtraw[:, s0:s0 + w], pj[:, :w]), deps=[mk])
                    else:
                        mguard[j % 2] = v.op(
                            lambda e, i=i, s0=s0, w=w, pj=pj: e.tensor_copy(
                                xr[i][:, NB + s0:NB + s0 + w], pj[:, :w]),
                            deps=[mk, xr_free[i]])
                if is_dt:
                    continue
                dgt = []
                for sh in range(KCONV):
                    dgt.append(v.op(lambda e, sh=sh, q=q: e.tensor_scalar(
                        diag[:, sh * 128:(sh + 1) * 128], self.ident_b[:, :],
                        self.cw_s[:, q * 4 + sh:q * 4 + sh + 1], None, ALU.mult),
                        deps=[self.cinit, self.pconst, dg_free]))
                xr_ready = [bguard, mguard[0], mguard[1]]
                for j, (s0, w) in enumerate(self.mba):
                    u2 = j % 2
                    for sh in range(KCONV):
                        pe.mm(p_c[u2][:, :w], diag[:, sh * 128:(sh + 1) * 128],
                              xr[i][:, s0 + sh: s0 + sh + w],
                              start=(sh == 0), stop=(sh == KCONV - 1),
                              deps=xr_ready + dgt + [cguard[u2]])
                    cmk = pe.mark()
                    atk = a.op(lambda e, u2=u2, q=q, w=w: e.activation(
                        hbct[u2][:, :w], p_c[u2][:, :w], AF.Silu,
                        bias=self.cb_s[:, q:q + 1]), deps=[cmk, hb_free[u2]])
                    cguard[u2] = atk
                    hb_free[u2] = qst.go(
                        f"hb{u2}", self.hbc_dram[q * 128:(q + 1) * 128, s0:s0 + w],
                        hbct[u2][:, :w], deps=[atk])
                    hbc_done.append(hb_free[u2])
                xr_free[i] = pe.tok()
                dg_free = pe.tok()
            self.hbc_done = [self.qs.tok("hb0"), self.qs.tok("hb1")]

            self.barrier(dma=False)
            st_conv.close()
            t_a = self.sb("s2ta", [128, self.TOKA], F32, st)
            t_b = self.sb("s2tb", [128, self.TOKA], F32, st)
            acs2 = self.sb("acs2", [128, self.TOKA], F32, st)
            dtg = [mguard[0], mguard[1]]
            v.op(lambda e: e.tensor_scalar(t_b[:, :], dtraw[:, :],
                                           self.dtb_s[:, 0:1], None, ALU.add),
                 deps=dtg + [self.pconst])
            a.op(lambda e: e.activation(t_a[:, :], t_b[:, :], AF.Abs),
                 deps=[v.tok()])
            a.op(lambda e: e.activation(t_a[:, :], t_a[:, :], AF.Exp, scale=-1.0))
            a.op(lambda e: e.activation(t_a[:, :], t_a[:, :], AF.Ln,
                                        bias=self.konst[:, 1:2]))
            v.op(lambda e: e.tensor_scalar(t_b[:, :], t_b[:, :], 0.0, None, ALU.max),
                 deps=[a.tok()])
            dspt = v.op(lambda e: e.tensor_add(self.dtsp[:, :], t_a[:, :], t_b[:, :]))
            a.op(lambda e: e.activation(self.alog_s[:, 1:2], self.alog_s[:, 0:1],
                                        AF.Exp), deps=[self.pconst])
            nat = v.op(lambda e: e.tensor_scalar(self.alog_s[:, 2:3],
                                                 self.alog_s[:, 1:2], -1.0, None,
                                                 ALU.mult), deps=[a.tok()])
            v.op(lambda e: e.tensor_scalar(acs2[:, :], self.dtsp[:, :],
                                           self.alog_s[:, 2:3], None, ALU.mult),
                 deps=[dspt, nat])
            cur, oth = acs2, self.acs
            for sh in [1, 2, 4, 8, 16, 32, 64]:
                c3 = cur[:, :].rearrange("p (c l) -> p c l", l=CH)
                o3 = oth[:, :].rearrange("p (c l) -> p c l", l=CH)
                v.op(lambda e, c3=c3, o3=o3, sh=sh: e.tensor_copy(
                    o3[:, :, 0:sh], c3[:, :, 0:sh]))
                v.op(lambda e, c3=c3, o3=o3, sh=sh: e.tensor_add(
                    o3[:, :, sh:CH], c3[:, :, sh:CH], c3[:, :, 0:CH - sh]))
                cur, oth = oth, cur
            assert cur is self.acs
            acst = v.tok()
            cst = v.op(lambda e: e.tensor_copy(
                self.csum[:, :], self.acs[:, CH - 1:self.TOKA:CH]))
            a.op(lambda e: e.activation(ea_b[:, :], self.csum[:, :], AF.Exp),
                 deps=[cst])
            eat = self.a.tok()
            qst.go("ea", self.ea_dram[:, :].transpose([1, 0]), ea_b[:, :],
                   deps=[eat], allow_slow_non_contiguous=True)
            self.aftok = {}
            for c in range(self.NCHA):
                self.aftok[c] = qst.go(
                    "af", self.acsflat_dram[c:c + 1, :].rearrange(
                        "a (h l) -> (a h) l", l=CH),
                    self.acs[:, c * CH:(c + 1) * CH], deps=[acst], fresh=False)
        self.tap("hbc", self.hbc_dram[:, :], [CONVD, self.TOKA], BF16,
                 self.hbc_done)
        self.tap("dtsp", self.dtsp[:, :], [128, self.TOKA], F32, [dspt])
        self.tap("acs", self.acs[:, :], [128, self.TOKA], F32, [acst])

    def s3a_ssm_local(self):
        TOK, T, NCH = self.TOK, self.T, self.NCH
        v, a, p, pe = self.v, self.a, self.p, self.pe
        qld, qst, qg = self.qs, self.qs, self.qg
        with ExitStack() as st:
            hbcL = [self.sb(f"s3L{i}", [128, 80 * 128], BF16, st) for i in range(2)]
            hc = self.sb("s3hc", [128, DIN], BF16, st)
            B_tm = self.sb("s3btm", [128, 8 * 128], BF16, st)
            tmq = self.sb("s3tmq", [128, 512], F32, st)
            earow = self.sb("s3ear", [1, 128], BF16, st)
            arep = self.sb("s3arep", [128, 128], F32, st)
            G_m = self.sb("s3gm", [128, 128], BF16, st)
            rep = [self.sb(f"s3rep{i}", [128, 2048], F32, st) for i in range(2)]
            emin = self.sb("s3emin", [128, 2048], BF16, st)
            Mt = [self.sb(f"s3mt{i}", [128, 2048], BF16, st) for i in range(2)]
            dhcs = [self.sb(f"s3dh{i}", [128, 512], BF16, st) for i in range(2)]
            pbfs = [self.sb(f"s3pb{i}", [128, 512], BF16, st) for i in range(2)]
            t1 = self.sb("s3t1", [128, 512], F32, st)
            t2 = self.sb("s3t2", [128, 512], F32, st)
            tmpa = self.sb("s3tma", [128, 128], F32, st)
            tmpb = self.sb("s3tmb", [128, 128], F32, st)
            ytm_buf = self.sb("s3ytm", [128, DIN], BF16, st)
            yfmb = self.sb("s3yfm", [128, 32 * 128], BF16, st)
            yfm_sp = [None, None]
            P = self.sb("s3P", [128, DIN], F32, st)
            pinit = p.op(lambda e: e.memset(P[:, :], 0.0))

            def load_hbc(c):
                i = c % 2
                src3 = self.hbc_dram.rearrange("(q p) t -> p q t", p=128)
                tk = None
                for part in range(8):
                    tk = qld.go(f"hbc{i}",
                                hbcL[i][:, part * 1280:(part + 1) * 1280].rearrange(
                                    "p (q t) -> p q t", t=CH),
                                src3[:, part * 10:(part + 1) * 10, c * CH:(c + 1) * CH],
                                deps=[self.hbc_done if c < 2 else None],
                                fresh=(part == 0))
                return tk

            ld_tok = [None, None]
            ld_tok[0] = load_hbc(0)
            rep_tok = [None, None]
            NCHA, NPRE = self.NCHA, self.NPRE
            for c in range(NCHA):
                own = c >= NPRE
                i = c % 2
                if c + 1 < NCHA:
                    ld_tok[(c + 1) % 2] = load_hbc(c + 1)
                ea_t = qld.go("ea", earow[:, :], self.ea_dram[c:c + 1, :],
                              deps=[pe.tok()])
                sl = slice(c * CH, (c + 1) * CH)
                dct = v.op(lambda e: e.tensor_scalar(
                    tmpa[:, :], self.acs[:, sl], self.acs[:, c * CH + CH - 1:c * CH + CH],
                    None, ALU.subtract))
                a.op(lambda e: e.activation(tmpa[:, :], tmpa[:, :], AF.Exp, scale=-1.0),
                     deps=[dct])
                ext = a.op(lambda e: e.activation(tmpb[:, :], self.acs[:, sl], AF.Exp))
                with ExitStack() as stA:
                    p_trf = self.ps("p_trf", [128, 512], st=stA)
                    p_tr4 = self.ps("p_tr4", [128, 512], BF16, st=stA)
                    pe.tr(p_trf[:, 0:128], self.dtsp[:, sl], self.ident_f[:, :],
                          deps=[self.cinit, v.tok(), a.tok()])
                    pe.tr(p_trf[:, 128:256], tmpa[:, :], self.ident_f[:, :])
                    pe.tr(p_trf[:, 256:384], tmpb[:, :], self.ident_f[:, :])
                    pe.tr(p_trf[:, 384:512], self.acs[:, sl], self.ident_f[:, :])
                    tmk = pe.mark()
                    tq = v.op(lambda e: e.tensor_copy(tmq[:, :], p_trf[:, :]), deps=[tmk])
                    hct = None
                    evg = None
                    for qb in range(0, 64, 4):
                        for u in range(4):
                            pe.tr(p_tr4[:, u * 128:(u + 1) * 128],
                                  hbcL[i][:, (qb + u) * 128:(qb + u + 1) * 128],
                                  self.ident_b[:, :],
                                  deps=[ld_tok[i], evg])
                        tmk = pe.mark()
                        evg = v.op(lambda e, qb=qb: e.tensor_tensor(
                            hc[:, qb * 128:(qb + 4) * 128].rearrange(
                                "p (h m) -> p h m", m=64),
                            p_tr4[:, :].rearrange("p (h m) -> p h m", m=64),
                            tmq[:, 2 * qb:2 * qb + 8].unsqueeze(2).broadcast_to(
                                [128, 8, 64]),
                            ALU.mult), deps=[tmk, tq])
                    for grp in range(2):
                        for u in range(4):
                            pe.tr(p_tr4[:, u * 128:(u + 1) * 128],
                                  hbcL[i][:, (64 + grp * 4 + u) * 128:
                                          (65 + grp * 4 + u) * 128],
                                  self.ident_b[:, :], deps=[evg])
                        tmk = pe.mark()
                        evg = v.op(lambda e, grp=grp: e.tensor_copy(
                            B_tm[:, grp * 512:(grp + 1) * 512], p_tr4[:, :]),
                            deps=[tmk])
                    hc_tok = v.tok()
                with ExitStack() as stB:
                    p_ar = self.ps("p_ar", [128, 128], st=stB)
                    p_g = self.ps("p_g", [128, 128], st=stB)
                    p_d = [self.ps(f"p_d{u}", [128, 512], st=stB) for u in range(2)]
                    p_cp = [self.ps(f"p_cp{u}", [128, 512], st=stB) for u in range(2)]
                    p_s = [self.ps(f"p_s{u}", [128, 512], st=stB) for u in range(2)]
                    pe.mm(p_ar[:, :], self.ones_b[0:1, 0:128], earow[:, :],
                          start=True, stop=True, deps=[ea_t, hc_tok])
                    amk = pe.mark()
                    ar_t = v.op(lambda e: e.tensor_copy(arep[:, :], p_ar[:, :]),
                                deps=[amk])
                    gev = None
                    mt_done = [None, None]
                    hg_guard = [None, None]
                    ysp_tok = None

                    def build_L(g):
                        nonlocal gev
                        pe.mm(p_g[:, :], hbcL[i][:, (64 + g) * 128:(65 + g) * 128],
                              hbcL[i][:, (72 + g) * 128:(73 + g) * 128],
                              start=True, stop=True, deps=[ld_tok[i], gev])
                        gmk = pe.mark()
                        gev = v.op(lambda e: e.tensor_tensor(
                            G_m[:, :], p_g[:, :], self.upmask_b[:, :], ALU.mult),
                            deps=[gmk, self.cinit])
                        rp = qld.go(f"rep{g % 2}", rep[g % 2][:, :],
                                    self.acsflat_dram[c:c + 1,
                                                      g * 2048:(g + 1) * 2048
                                                      ].broadcast_to([128, 2048]),
                                    deps=[self.aftok[c], v.tok()])
                        v.op(lambda e, g=g: e.tensor_tensor(
                            rep[g % 2][:, :].rearrange("p (h l) -> p h l", l=CH),
                            rep[g % 2][:, :].rearrange("p (h l) -> p h l", l=CH),
                            tmq[:, 384 + 16 * g:384 + 16 * g + 16].unsqueeze(
                                2).broadcast_to([128, 16, 128]),
                            ALU.subtract), deps=[rp, tq])
                        v.op(lambda e, g=g: e.tensor_scalar(
                            emin[:, :], rep[g % 2][:, :], 0.0, None, ALU.min))
                        a.op(lambda e: e.activation(emin[:, :], emin[:, :], AF.Exp),
                             deps=[v.tok()])
                        mt_done[g % 2] = v.op(lambda e, g=g: e.tensor_tensor(
                            Mt[g % 2][:, :].rearrange("p (h l) -> p h l", l=CH),
                            emin[:, :].rearrange("p (h l) -> p h l", l=CH),
                            G_m[:, :].unsqueeze(1).broadcast_to([128, 16, 128]),
                            ALU.mult), deps=[a.tok(), pe.tok()])

                    def heads(g):
                        for hgl in range(2):
                            hg = 2 * g + hgl
                            u = hg % 2
                            h0 = hg * 8
                            stg = []
                            for hh in range(8):
                                h_ = h0 + hh
                                stg.append(v.op(lambda e, hh=hh, h_=h_, u=u:
                                    e.tensor_scalar(
                                        dhcs[u][:, hh * 64:(hh + 1) * 64],
                                        hc[:, h_ * 64:(h_ + 1) * 64],
                                        tmq[:, 128 + h_:129 + h_], None, ALU.mult),
                                    deps=[hc_tok, hg_guard[u]]))
                            if own:
                                stg.append(v.op(lambda e, h0=h0, u=u: e.tensor_copy(
                                    pbfs[u][:, :], P[:, h0 * 64:(h0 + 8) * 64]),
                                    deps=[hg_guard[u], pinit]))
                                for hh in range(8):
                                    h_ = h0 + hh
                                    hhg = h_ - g * 16
                                    pe.mm(p_d[u][:, hh * 64:(hh + 1) * 64],
                                          Mt[g % 2][:, hhg * 128:(hhg + 1) * 128],
                                          hc[:, h_ * 64:(h_ + 1) * 64],
                                          start=True, stop=True,
                                          deps=[mt_done[g % 2], hc_tok,
                                                hg_guard[u]])
                                for hh in range(8):
                                    pe.mm(p_cp[u][:, hh * 64:(hh + 1) * 64],
                                          hbcL[i][:, (72 + g) * 128:(73 + g) * 128],
                                          pbfs[u][:, hh * 64:(hh + 1) * 64],
                                          start=True, stop=True, deps=stg)
                            for hh in range(8):
                                pe.mm(p_s[u][:, hh * 64:(hh + 1) * 64],
                                      B_tm[:, g * 128:(g + 1) * 128],
                                      dhcs[u][:, hh * 64:(hh + 1) * 64],
                                      start=True, stop=True, deps=stg)
                            hmk = pe.mark()
                            if own:
                                v.op(lambda e, u=u, h0=h0: e.tensor_tensor(
                                    t1[:, :].rearrange("p (h m) -> p h m", m=64),
                                    p_cp[u][:, :].rearrange("p (h m) -> p h m", m=64),
                                    tmq[:, 256 + h0:256 + h0 + 8].unsqueeze(
                                        2).broadcast_to([128, 8, 64]),
                                    ALU.mult), deps=[hmk, ysp_tok, tq])
                                v.op(lambda e, u=u, hg=hg: e.tensor_add(
                                    ytm_buf[:, hg * 512:(hg + 1) * 512], t1[:, :],
                                    p_d[u][:, :]))
                            v.op(lambda e, u=u, h0=h0: e.tensor_tensor(
                                t2[:, :].rearrange("p (h m) -> p h m", m=64),
                                P[:, h0 * 64:(h0 + 8) * 64].rearrange(
                                    "p (h m) -> p h m", m=64),
                                arep[:, h0:h0 + 8].unsqueeze(2).broadcast_to(
                                    [128, 8, 64]),
                                ALU.mult), deps=[ar_t, hmk, pinit])
                            hg_guard[u] = v.op(lambda e, u=u, h0=h0: e.tensor_add(
                                P[:, h0 * 64:(h0 + 8) * 64], t2[:, :], p_s[u][:, :]),
                                deps=[pinit])
                            if not own:
                                hg_guard[u] = v.op(
                                    lambda e, h0=h0, c=c: e.tensor_scalar(
                                        P[:, h0 * 64:(h0 + 8) * 64],
                                        P[:, h0 * 64:(h0 + 8) * 64],
                                        self.pmask_s[:, c:c + 1], None, ALU.mult),
                                    deps=[self.pconst])

                    if own:
                        for g in range(GM):
                            build_L(g)
                            if g >= 1:
                                heads(g - 1)
                        heads(GM - 1)
                    else:
                        for g in range(GM):
                            heads(g)
                if not own:
                    self.barrier(dma=False)
                    continue
                co = c - NPRE
                with ExitStack() as stC:
                    p_tb = self.ps("p_tb", [128, 512], BF16, st=stC)
                    dst3 = self.yfm_dram.rearrange("(q p) t -> p q t", p=128)
                    tguard = None
                    for half in range(2):
                        for qq in range(0, 32, 4):
                            qb = half * 32 + qq
                            for uu in range(4):
                                pe.tr(p_tb[:, uu * 128:(uu + 1) * 128],
                                      ytm_buf[:, (qb + uu) * 128:(qb + uu + 1) * 128],
                                      self.ident_b[:, :],
                                      deps=[v.tok(), tguard])
                            tmk = pe.mark()
                            for uu in range(4):
                                tguard = v.op(
                                    lambda e, qq=qq, qb=qb, uu=uu:
                                    e.scalar_tensor_tensor(
                                        yfmb[:, (qq + uu) * 128:(qq + uu + 1) * 128],
                                        hbcL[i][:, (qb + uu) * 128:(qb + uu + 1) * 128],
                                        self.dpx_s[:, qb + uu:qb + uu + 1],
                                        p_tb[:, uu * 128:(uu + 1) * 128],
                                        ALU.mult, ALU.add),
                                    deps=[tmk, ld_tok[i], self.pconst,
                                          yfm_sp[0], yfm_sp[1]])
                        for part in range(2):
                            yfm_sp[half] = qst.go(
                                f"yfm{half}",
                                dst3[:, half * 32 + part * 16:
                                     half * 32 + (part + 1) * 16,
                                     co * CH:(co + 1) * CH],
                                yfmb[:, part * 2048:(part + 1) * 2048].rearrange(
                                    "p (q t) -> p q t", t=CH),
                                deps=[v.tok()], fresh=(part == 0))
                self.barrier(dma=False)

    def s3b_ssm_corr(self):
        pass

    def s4_gate(self):
        TOK, T = self.TOK, self.T
        v, a, p, pe = self.v, self.a, self.p, self.pe
        qld, qst, qw = self.qs, self.qs, self.qg
        with ExitStack() as st:
            act2 = self.sb("act2", [128, DC * TOK], BF16, st, side="right")
            h4 = None
            for dc in range(DC):
                h4 = qld.go(f"h4{dc % 2}", act2[:, dc * TOK:(dc + 1) * TOK],
                            self.h_dram[dc * 128:(dc + 1) * 128, self.OWN0:],
                            deps=[self.h_spilled] if dc < 2 else [])
            h4tok = [self.qs.tok("h40"), self.qs.tok("h41")]
            wbuf = [self.sb(f"s4w{i}", [128, DC * 128], BF16, st) for i in range(2)]
            ybuf = [self.sb(f"s4y{i}", [128, TOK], BF16, st) for i in range(2)]
            ygt = [self.sb(f"s4g{i}", [128, TOK], BF16, st) for i in range(2)]
            silt = self.sb("s4sil", [128, 512], BF16, st)
            yg2 = self.sb("s4q", [128, 512], BF16, st)
            oneh = self.sb("s4oh", [128, 8 * 8], BF16, st)
            rsq = self.sb("s4rsq", [8, TOK], F32, st)
            oht = p.op(lambda e: e.memset(oneh[:, :], 0.0))
            for g in range(8):
                oht = p.op(lambda e, g=g: e.affine_select(
                    out=oneh[:, g * 8:(g + 1) * 8], in_=oneh[:, g * 8:(g + 1) * 8],
                    compare_op=ALU.not_equal, fill=1.0, base=-g,
                    pattern=[[1, 8]], channel_multiplier=0), deps=[oht])
            p_g = [self.ps(f"p_g4{i}", [128, 512], st=st) for i in range(2)]
            p_st = self.ps("p_st4", [8, TOK], st=st)
            wtok = [None, None]
            wtok[0] = qw.go("w0", wbuf[0][:, :], self.win[0], deps=[pe.tok()])
            sguard = [None, None]
            ytok = [None, None]
            ygsp = [None, None]
            first_st = True
            for oc in range(64):
                i = oc % 2
                if oc + 1 < 64:
                    wtok[(oc + 1) % 2] = qw.go(f"w{(oc + 1) % 2}",
                                               wbuf[(oc + 1) % 2][:, :],
                                               self.win[oc + 1], deps=[pe.tok()])
                ytok[i] = qld.go(f"s4y{i}", ybuf[i][:, :],
                                 self.yfm_dram[oc * 128:(oc + 1) * 128, :],
                                 deps=[self.qs.tok("yfm"), v.tok()])
                g = oc // 8
                for j, (s0, w) in enumerate(self.mb):
                    pj = p_g[j % 2]
                    for k in range(DC):
                        pe.mm(pj[:, :w], wbuf[i][:, k * 128:(k + 1) * 128],
                              act2[:, k * TOK + s0: k * TOK + s0 + w],
                              start=(k == 0), stop=(k == DC - 1),
                              deps=[wtok[i], sguard[j % 2], h4tok])
                    mk = pe.mark()
                    at = a.op(lambda e, s0=s0, w=w, pj=pj: e.activation(
                        silt[:, :w], pj[:, :w], AF.Silu), deps=[mk, v.tok()])
                    vt = v.op(lambda e, i=i, s0=s0, w=w: e.tensor_mul(
                        ygt[i][:, s0:s0 + w], silt[:, :w], ybuf[i][:, s0:s0 + w]),
                        deps=[at, ytok[i], ygsp[i]])
                    sguard[j % 2] = vt
                    q2 = v.op(lambda e, i=i, s0=s0, w=w: e.tensor_mul(
                        yg2[:, :w], ygt[i][:, s0:s0 + w], ygt[i][:, s0:s0 + w]))
                    pe.mm(p_st[:, s0:s0 + w], oneh[:, g * 8:(g + 1) * 8], yg2[:, :w],
                          start=first_st and j < len(self.mb), stop=(oc == 63),
                          deps=[q2, oht])
                    pe.mark()
                first_st = False
                ygsp[i] = qst.go(f"yg{i}", self.ygn_dram[oc * 128:(oc + 1) * 128, :],
                                 ygt[i][:, :], deps=[v.tok()])
            smk = pe.tok()
            a.op(lambda e: e.activation(rsq[:, :], p_st[:, :], AF.Ln,
                                        scale=1.0 / 1024.0,
                                        bias=self.konst[0:8, 0:1]), deps=[smk])
            rt = a.op(lambda e: e.activation(rsq[:, :], rsq[:, :], AF.Exp, scale=-0.5))
            qst.go("rsq", self.rsq_dram[:, :], rsq[:, :], deps=[rt])
        if "yg" in self.taps:
            ap = self.outp("tap_yg", [DIN, TOK], BF16)
            self.qs.go("tap", ap[:, :], self.ygn_dram[:, :],
                       deps=[self.qs.tok("yg0"), self.qs.tok("yg1")], fresh=False)
        if "rsq" in self.taps:
            ap = self.outp("tap_rsq", [8, TOK], F32)
            self.qs.go("tap", ap[:, :], self.rsq_dram[:, :],
                       deps=[self.qs.tok("rsq")], fresh=False)

    def s5_outproj(self):
        TOK, T = self.TOK, self.T
        v, a, p, pe = self.v, self.a, self.p, self.pe
        qld, qst, qw = self.qs, self.qs, self.qg
        QT = min(512, self.TOK)
        nq = TOK // QT
        with ExitStack() as st:
            ygb = self.sb("ygb", [128, 64 * QT], BF16, st, side="right")
            bmat_b = self.sb("bmat_b", [32, D], BF16, st)
            self.bff_b = self.sb("bff_b", [1, D], BF16, st)
            qw.go("wc", bmat_b[:, :], self.bmat[:, :], fresh=False)
            qw.go("wc", self.bff_b[:, :], self.bff[:, :], fresh=False)
            bmt = self.qg.tok("wc")
            wop_b = [self.sb(f"s5w{i}", [128, DIN], BF16, st) for i in range(2)]
            rep_g = self.sb("rep_g", [128, QT], F32, st)
            s1_rep = self.sb("s1_rep5", [128, QT], F32, st)
            msr = self.sb("ms_rep5", [128, QT], F32, st)
            tq5 = self.sb("tq5", [128, QT], BF16, st)
            rrow = self.sb("rrow", [1, QT], F32, st)
            xb = [self.sb(f"s5x{i}", [128, 512], F32, st) for i in range(2)]
            Hb = [self.sb(f"s5h{i}", [128, 512], BF16, st) for i in range(2)]
            t3 = self.sb("s5t3", [128, 512], F32, st)
            outst = [self.sb(f"s5o{i}", [128, QT], F32, st) for i in range(2)]
            p_o = [self.ps(f"p_o{i}", [128, 512], st=st) for i in range(2)]
            p_r5 = self.ps("p_r5", [128, 512], st=st)
            wtok = [None, None]
            oguard = [None, None]
            ost = [None, None]
            for q in range(nq):
                t0 = q * QT
                src3 = self.ygn_dram.rearrange("(kc p) t -> p kc t", p=128)
                ygt = None
                for part in range(8):
                    ygt = qld.go("ygq", ygb[:, part * 8 * QT:(part + 1) * 8 * QT
                                           ].rearrange("p (kc t) -> p kc t", t=QT),
                                 src3[:, part * 8:(part + 1) * 8, t0:t0 + QT],
                                 deps=[self.qs.tok("yg0"), self.qs.tok("yg1"),
                                       pe.tok(), v.tok()] if part == 0 else [],
                                 fresh=(part == 0))
                rt = self._replicate_rows(
                    p_r5, [(self.s1row, s1_rep), (self.msrow, msr)],
                    [(0, QT)], deps=[p.tok()], off=self.OWN0 + t0)
                sc_done = []
                for g in range(8):
                    rr = qld.go("rr", rrow[:, :], self.rsq_dram[g:g + 1, t0:t0 + QT],
                                deps=[self.qs.tok("rsq"), pe.tok()])
                    pe.mm(p_r5[:, :QT], self.ones_f[0:1, :], rrow[:, :],
                          start=True, stop=True, deps=[rr, v.tok()])
                    rmk = pe.mark()
                    rg = v.op(lambda e: e.tensor_copy(rep_g[:, :], p_r5[:, :QT]),
                              deps=[rmk])
                    for kc in range(g * 8, (g + 1) * 8):
                        v.op(lambda e, kc=kc: e.tensor_mul(
                            tq5[:, :], ygb[:, kc * QT:(kc + 1) * QT], rep_g[:, :]),
                            deps=[ygt, rg])
                        sc_done.append(v.op(lambda e, kc=kc: e.tensor_scalar(
                            ygb[:, kc * QT:(kc + 1) * QT], tq5[:, :],
                            self.gnw_s[:, kc:kc + 1], None, ALU.mult)))
                if q == 0:
                    wtok[0] = qw.go("w0", wop_b[0][:, :], self.wop[0], deps=[pe.tok()])
                for oc in range(DC):
                    i = oc % 2
                    if oc + 1 < DC:
                        wtok[(oc + 1) % 2] = qw.go(f"w{(oc + 1) % 2}",
                                                   wop_b[(oc + 1) % 2][:, :],
                                                   self.wop[oc + 1], deps=[pe.tok()])
                    elif q + 1 < nq:
                        wtok[(oc + 1) % 2] = qw.go(f"w{(oc + 1) % 2}",
                                                   wop_b[(oc + 1) % 2][:, :],
                                                   self.wop[0], deps=[pe.tok()])
                    for jb in range(0, QT, 512):
                        wj = min(512, QT - jb)
                        u = (oc + jb // 512) % 2
                        pj = p_o[u]
                        for k in range(64):
                            pe.mm(pj[:, :wj], wop_b[i][:, k * 128:(k + 1) * 128],
                                  ygb[:, k * QT + jb:k * QT + jb + wj],
                                  start=(k == 0), stop=False,
                                  deps=[wtok[i], oguard[u]] + sc_done)
                        pe.mm(pj[:, :wj], bmat_b[:, oc * 128:(oc + 1) * 128],
                              self.zp[:, t0 + jb:t0 + jb + wj],
                              start=False, stop=False, deps=[bmt, self.zp_done])
                        pe.mm(pj[:, :wj], self.bff_b[0:1, oc * 128:(oc + 1) * 128],
                              self.ones_b[0:1, 0:wj],
                              start=False, stop=True, deps=[bmt])
                        mk = pe.mark()
                        xt = qld.go(f"s5x{u}", xb[u][:, :wj],
                                    self.x[oc * 128:(oc + 1) * 128,
                                           self.OWN0 + t0 + jb:
                                           self.OWN0 + t0 + jb + wj],
                                    deps=[p.tok()])
                        htk = qld.go(f"s5h{u}", Hb[u][:, :wj],
                                     self.H_dram[oc * 128:(oc + 1) * 128,
                                                 t0 + jb:t0 + jb + wj],
                                     deps=[self.s1_h_done, v.tok()] if q == 0 and oc < 2
                                     else [v.tok()])
                        p.op(lambda e, u=u, jb=jb, wj=wj: e.tensor_mul(
                            xb[u][:, :wj], xb[u][:, :wj], s1_rep[:, jb:jb + wj]),
                            deps=[xt, rt])
                        pt = p.op(lambda e, u=u, jb=jb, wj=wj: e.tensor_sub(
                            xb[u][:, :wj], xb[u][:, :wj], msr[:, jb:jb + wj]))
                        v.op(lambda e, u=u, pj=pj, wj=wj: e.tensor_add(
                            t3[:, :wj], xb[u][:, :wj], pj[:, :wj]), deps=[mk, pt])
                        oguard[u] = v.op(lambda e, i=i, u=u, jb=jb, wj=wj: e.tensor_add(
                            outst[i][:, jb:jb + wj], t3[:, :wj], Hb[u][:, :wj]),
                            deps=[htk, ost[i]])
                    ost[i] = self.qs.go(f"out{i}", self.out[oc * 128:(oc + 1) * 128,
                                                            t0:t0 + QT],
                                        outst[i][:, :QT], deps=[v.tok()])

    def finish(self):
        for q in (self.qs, self.qg):
            for name, (sem, n) in q.slots.items():
                if n:
                    q.e.wait_ge(sem, n)
                    q.w[sem.num] = n
        self._root.close()



def _block_weight(wt):
    K, O = wt.shape
    a = wt.reshape(K // 128, 128, O // 128, 128)
    a = a.transpose(2, 1, 0, 3)
    return np.ascontiguousarray(a.reshape(O // 128, 128, K)).astype(np.float32)


def prep_core_inputs(core_id, x_full, tok=2048, n_cores=8, seq_cores=4, pre=128):
    b = core_id // seq_cores
    t0 = (core_id % seq_cores) * tok
    xs = x_full[b].T
    lead = pre + NB
    lo = t0 - lead
    padn = max(0, -lo)
    x_fm = xs[:, max(0, lo):t0 + tok]
    if padn:
        x_fm = np.concatenate([np.zeros((D, padn), np.float32), x_fm], 1)
    pm = np.zeros((128, 4), np.float32)
    for c in range(pre // CH):
        pm[:, c] = 1.0 if (t0 - pre + c * CH) >= 0 else 0.0
    return {"x": np.ascontiguousarray(x_fm).astype(np.float32), "pmask": pm}


def prep_shared_inputs(prm, x_full):
    dpq = prm["Dp"][(np.arange(DIN) // PM)]
    xa = np.concatenate([x_full[b].T[:, 1::2] for b in range(x_full.shape[0])], 1)
    w8 = np.ascontiguousarray(prm["W"][:8].T).reshape(DC, 128, 8)
    aa = prm["A"].reshape(DC, 128, 32)
    w8a = np.concatenate([w8, aa], axis=2).transpose(1, 0, 2).reshape(128, DC * 40)
    return {
        "xa": np.ascontiguousarray(xa).astype(np.float32),
        "w8a": np.ascontiguousarray(w8a).astype(np.float32),
        "win": _block_weight(np.ascontiguousarray(prm["in_proj_w"].T)),
        "wffn": _block_weight(np.ascontiguousarray(prm["W"].T)),
        "wop": _block_weight(np.ascontiguousarray(prm["out_proj_w"].T)),
        "alora": np.ascontiguousarray(
            prm["A"].reshape(DC, 128, 32).transpose(1, 0, 2).reshape(128, DC * 32)
        ).astype(np.float32),
        "bmat": prm["Bmat"].astype(np.float32),
        "limes": prm["LiMEs"].astype(np.float32),
        "limest": np.ascontiguousarray(prm["LiMEs"].T).astype(np.float32),
        "nw": np.ascontiguousarray(prm["norm_w"].reshape(DC, 128).T).astype(np.float32),
        "gnw": np.ascontiguousarray(prm["gnorm_w"].reshape(64, 128).T).astype(np.float32),
        "cw": np.ascontiguousarray(
            prm["conv_w"].reshape(80, 128, 4).transpose(1, 0, 2).reshape(128, 320)
        ).astype(np.float32),
        "cb": np.ascontiguousarray(prm["conv_b"].reshape(80, 128).T).astype(np.float32),
        "dtb": prm["dt_bias"].reshape(128, 1).astype(np.float32),
        "alog": prm["A_log"].reshape(128, 1).astype(np.float32),
        "dpx": np.ascontiguousarray(dpq.reshape(64, 128).T).astype(np.float32),
        "bff": prm["b"].reshape(1, D).astype(np.float32),
        "bcol": np.ascontiguousarray(prm["b"].reshape(DC, 128).T).astype(np.float32),
    }


def kernel(L0_attn_x, L0_attn_norm_w, L0_attn_in_proj_w, L0_attn_conv_w, L0_attn_conv_b,
           L0_attn_dt_bias, L0_attn_A_log, L0_attn_Dp, L0_attn_gnorm_w, L0_attn_out_proj_w,
           L0_ffn_W, L0_ffn_b, L0_ffn_A, L0_ffn_Bmat, L0_ffn_LiMEs):
    prm = {
        "norm_w": np.asarray(L0_attn_norm_w), "in_proj_w": np.asarray(L0_attn_in_proj_w),
        "conv_w": np.asarray(L0_attn_conv_w), "conv_b": np.asarray(L0_attn_conv_b),
        "dt_bias": np.asarray(L0_attn_dt_bias), "A_log": np.asarray(L0_attn_A_log),
        "Dp": np.asarray(L0_attn_Dp), "gnorm_w": np.asarray(L0_attn_gnorm_w),
        "out_proj_w": np.asarray(L0_attn_out_proj_w), "W": np.asarray(L0_ffn_W),
        "b": np.asarray(L0_ffn_b), "A": np.asarray(L0_ffn_A),
        "Bmat": np.asarray(L0_ffn_Bmat), "LiMEs": np.asarray(L0_ffn_LiMEs),
    }
    x_full = np.asarray(L0_attn_x)
    TOK = 2048
    shared = prep_shared_inputs(prm, x_full)
    in_maps = []
    for c in range(8):
        m = dict(shared)
        m.update(prep_core_inputs(c, x_full, TOK))
        in_maps.append(m)
    b = Build(tok=TOK, naa=8192).build()
    res = run_bass_kernel_spmd(b.nc, in_maps, core_ids=list(range(8)))
    outs = [np.asarray(res.results[c]["out"]).T for c in range(8)]
    return np.concatenate(outs, 0).reshape(2, 8192, D).astype(np.float32)
